# version 01 (final)
# baseline (speedup 1.0000x reference)
import numpy as np
import concourse.bass as bass
import concourse.mybir as mybir
from concourse.bass_utils import run_bass_kernel_spmd

F32 = mybir.dt.float32
BF16 = mybir.dt.bfloat16
AF = mybir.ActivationFunctionType
ALU = mybir.AluOpType
AX = mybir.AxisListType

TOK = 4096
DBG = False
SKIP_LN = False
G = 512
NG = TOK // G
NHEAD = 16

ENGINES = ("tensor", "vector", "scalar", "sync", "gpsimd")


class Sem:
    def __init__(self, handle, name):
        self.h = handle
        self.name = name
        self.n = 0


class Plan:
    def __init__(self, nc):
        self.nc = nc
        self.q = {k: [] for k in ENGINES}
        self._sems = []
        self.chain = {}
        self.dmapool = {}
        self.dmaidx = {}

    def sem(self, name):
        h = self.nc.semaphore(name).__enter__()
        s = Sem(h, name)
        self._sems.append(s)
        return s

    def w(self, eng, gate):
        if gate is None:
            return
        s, v = gate
        if v <= 0:
            return
        self.q[eng].append(lambda e, s=s, v=v: e.wait_ge(s.h, v))

    def i(self, eng, fn, inc=None):
        cs = self.chain.get(eng)
        if cs is not None:
            assert inc is None or inc[0] is cs, (eng, inc)
            self.w(eng, (cs, cs.n))
            if inc is None:
                inc = (cs, 1)
        if inc is not None:
            s, k = inc
            s.n += k
            v = s.n

            def run(e, fn=fn, s=s, k=k):
                fn(e).then_inc(s.h, k)
            self.q[eng].append(run)
            return (s, v)
        self.q[eng].append(fn)
        return None

    def dma(self, eng, fn):
        if eng not in self.dmapool:
            n = 10 if eng == "gpsimd" else 6
            self.dmapool[eng] = [self.sem(f"dp_{eng}{i}") for i in range(n)]
            self.dmaidx[eng] = 0
        pool = self.dmapool[eng]
        s = pool[self.dmaidx[eng] % len(pool)]
        self.dmaidx[eng] += 1
        self.w(eng, (s, s.n))
        return self.i(eng, fn, inc=(s, 16))

    def run(self):
        with self.nc.Block() as block:
            for name in ENGINES:
                ops = self.q[name]

                def body(eng, ops=ops):
                    for op in ops:
                        op(eng)
                getattr(block, name)(body)


def rearr(ap):
    return ap.rearrange("(kc p) t -> p kc t", p=128)


def sb3(tile, t):
    return tile[:].rearrange("p (kc t) -> p kc t", t=t)




def build_program(blocks):
    nc = bass.Bass("TRN2", target_bir_lowering=False)
    ctx = {"blocks": blocks}
    din = lambda name, shape: nc.dram_tensor(name, shape, F32, kind="ExternalInput").ap()

    sin = din("sin", [1024, TOK])
    sout = nc.dram_tensor("sout", [1024, TOK], F32, kind="ExternalOutput").ap()
    lo, hi = blocks[0], blocks[-1]
    cur = sin
    for b in blocks:
        ctx[f"in{b}"] = cur
        if b == hi:
            ctx[f"out{b}"] = sout
        else:
            cur = nc.dram_tensor(f"s{b+1}", [1024, TOK], F32, kind="Internal").ap()
            ctx[f"out{b}"] = cur

    ctx["ident_d"] = din("ident_d", [128, 128])
    if 0 in blocks:
        ctx["wqkvT"] = din("wqkvT", [1024, 3072])
        ctx["wprojT"] = din("wprojT", [1024, 1024])
    if 1 in blocks:
        ctx["routerT"] = din("routerT", [1024, 8])
        ctx["gateT"] = din("gateT", [8, 1024, 2816])
        ctx["upT"] = din("upT", [8, 1024, 2816])
        ctx["downT"] = din("downT", [8, 2816, 1024])
        ctx["sel8_d"] = din("sel8_d", [8, 1024])
    if 2 in blocks:
        ctx["wqT"] = din("wqT", [1024, 1024])
        ctx["wvT"] = din("wvT", [1024, 1024])
        ctx["qgam_d"] = din("qgam_d", [128, 8])
        ctx["vgam_d"] = din("vgam_d", [128, 1024])
        ctx["hsel_d"] = din("hsel_d", [128, 128])
        ctx["bsel_d"] = din("bsel_d", [16, 1024])
        ctx["mask_d"] = din("mask_d", [128, 128])
    if 3 in blocks:
        ctx["keysT"] = din("keysT", [1024, 16])
        ctx["rwT"] = din("rwT", [1024, 16])
        ctx["rb_d"] = din("rb_d", [1, 16])
        ctx["w1T"] = din("w1T", [16, 1024, 4096])
        ctx["w2T"] = din("w2T", [16, 4096, 1024])
        ctx["b1_d"] = din("b1_d", [128, 512])
        ctx["b2_d"] = din("b2_d", [16, 1024])
        ctx["sel16_d"] = din("sel16_d", [16, 2048])

    A = nc.alloc_sbuf_tensor
    sb = {}
    sb["ones1"] = A("ones1", [1, 128], F32)
    sb["ones128"] = A("ones128", [128, 1], F32)
    sb["ident_bf"] = A("ident_bf", [128, 128], BF16)
    sb["ln_sq"] = A("ln_sq", [128, 512], F32)
    sb["ln_bc1"] = A("ln_bc1", [128, 256], F32)
    sb["ln_bc2"] = A("ln_bc2", [128, 256], F32)
    sb["ln_out"] = A("ln_out", [128, 2048], F32)
    sb["ln_row"] = A("ln_row", [1, 1024], F32)
    sb["eps5"] = A("eps5", [128, 1], F32)
    sb["eps8"] = A("eps8", [128, 1], F32)
    for nm in ("col_a", "col_b", "col_c", "col_d", "col_e", "col_f"):
        sb[nm] = A(nm, [128, 1], F32)
    sb["p8_a"] = A("p8_a", [128, 16], F32)
    sb["p8_b"] = A("p8_b", [128, 16], F32)
    sb["p8_c"] = A("p8_c", [128, 16], F32)
    sb["wte_bf"] = A("wte_bf", [128, 16], BF16)
    if 1 in blocks:
        sb["routerT"] = A("routerT_sb", [128, 64], F32)
        sb["sel8"] = A("sel8", [8, 1024], BF16)
        sb["w_fm8"] = A("w_fm8", [8, 512], BF16)
    if 2 in blocks:
        sb["mask"] = A("mask", [128, 128], F32)
        sb["qgam"] = A("qgam", [128, 8], F32)
        sb["vgam"] = A("vgam", [128, 1024], F32)
        sb["hsel"] = A("hsel", [128, 128], F32)
        sb["bsel"] = A("bsel", [16, 1024], F32)
        sb["S_sb"] = A("S_sb", [64, 1024], F32)
        sb["S_bf"] = A("S_bf", [128, 1024], BF16)
    if 3 in blocks:
        sb["wcomb"] = A("wcomb", [128, 128], F32)
        sb["keys_sb"] = A("keys_sb", [128, 128], F32)
        sb["rw_sb"] = A("rw_sb", [128, 128], F32)
        sb["crow"] = A("crow", [1, 16], F32)
        sb["rb_sb"] = A("rb_sb", [1, 16], F32)
        sb["kkrow"] = A("kkrow", [1, 16], F32)
        sb["b1_all"] = A("b1_all", [128, 512], F32)
        sb["b2_bf"] = A("b2_bf", [16, 1024], BF16)
        sb["sel16"] = A("sel16", [16, 2048], BF16)
        sb["l_a"] = A("l_a", [128, 16], F32)
        sb["l_b"] = A("l_b", [128, 16], F32)
        sb["w_fm"] = A("w_fm", [16, 512], BF16)
        sb["wbc"] = A("wbc", [128, 512], BF16)

    TOPB = 229344
    base = TOPB - nc.sbuf_bytes_remaining
    base = (base + 31) & ~31
    SZ = dict(W=88 * 1024, H=32 * 1024, X=8 * 1024, Y=16 * 1024, M=18 * 1024)
    assert base + sum(SZ.values()) <= TOPB, base
    R_W = base
    R_H = R_W + SZ["W"]
    R_X = R_H + SZ["H"]
    R_Y = R_X + SZ["X"]
    R_M = R_Y + SZ["Y"]
    place = lambda name, shape, dt, off: nc.alloc_sbuf_tensor_at(name, shape, dt, offset=off)

    sb["x_bf"] = place("x_bf", [128, 4096], BF16, R_X)
    sb["y_sb"] = place("y_sb", [128, 4096], F32, R_Y)
    if 0 in blocks:
        sb["a_wqkv"] = place("a_wqkv", [128, 8 * 3072], BF16, R_W)
        sb["a_wproj"] = place("a_wproj", [128, 8 * 1024], BF16, R_W + 48 * 1024)
        sb["a_q"] = place("a_q", [128, 8 * 512], BF16, R_H)
        sb["a_k"] = place("a_k", [128, 8 * 512], BF16, R_H + 8 * 1024)
        sb["a_v"] = place("a_v", [128, 4 * 1024], BF16, R_H + 16 * 1024)
        sb["a_y"] = place("a_y", [128, 8 * 512], BF16, R_H + 24 * 1024)
        sb["a_p"] = place("a_p", [128, 256], BF16, R_M)
        sb["a_pt"] = place("a_pt", [128, 256], BF16, R_M + 512)
        sb["a_e"] = place("a_e", [128, 256], F32, R_M + 1024)
    if 1 in blocks:
        sb["f_gate"] = place("f_gate", [128, 8 * 2816], BF16, R_W)
        sb["f_up"] = place("f_up", [128, 8 * 2816], BF16, R_W + 44 * 1024)
        sb["f_down"] = place("f_down", [128, 22 * 1024], BF16, R_W + 44 * 1024)
        sb["f_h"] = place("f_h", [128, 22 * 512], BF16, R_H)
        sb["f_wbc"] = place("f_wbc", [128, 8 * 512], BF16, R_H + 22 * 1024)
        sb["f_sg"] = place("f_sg", [128, 2 * 512], F32, R_M)
        sb["f_t"] = place("f_t", [128, 2 * 512], F32, R_M + 4 * 1024)
    if 2 in blocks:
        sb["la_wq"] = place("la_wq", [128, 8 * 1024], BF16, R_W)
        sb["la_wv"] = place("la_wv", [128, 8 * 1024], BF16, R_W + 16 * 1024)
        sb["la_qn"] = place("la_qn", [128, 8 * 512], BF16, R_H)
        sb["la_v"] = place("la_v", [128, 4 * 1024], BF16, R_H + 8 * 1024)
        sb["la_xtm"] = place("la_xtm", [128, 4 * 1024], BF16, R_H + 16 * 1024)
        sb["la_qpre"] = place("la_qpre", [128, 8 * 512], BF16, R_M)
        sb["la_vf"] = place("la_vf", [128, 1024], F32, R_M + 8 * 1024)
        sb["la_s16"] = place("la_s16", [16, 1024], F32, R_M + 12 * 1024)
        sb["la_vs"] = place("la_vs", [128, 32], F32, R_M + 16 * 1024)
        sb["la_p"] = place("la_p", [128, 128], BF16, R_M + 16 * 1024 + 128)
        sb["la_pt"] = place("la_pt", [128, 128], BF16, R_M + 16 * 1024 + 384)
    if 3 in blocks:
        sb["g_w1"] = place("g_w1", [128, 8 * 4096], BF16, R_W)
        sb["g_h"] = place("g_h", [128, 32 * 512], BF16, R_H)
        sb["g_hg"] = place("g_hg", [128, 2 * 512], F32, R_M)

    ps = {}
    for nm in ("psA", "psB", "psC", "psD", "psE", "psF"):
        ps[nm] = nc.alloc_psum_tensor(nm, [128, 512], F32)
    ps["psT"] = nc.alloc_psum_tensor("psT", [128, 128], BF16)
    ps["psU"] = nc.alloc_psum_tensor("psU", [128, 512], F32)

    if DBG and 0 in blocks:
        ctx["dbg_q"] = nc.dram_tensor("dbg_q", [128, 4096], F32, kind="ExternalOutput").ap()
        ctx["dbg_k"] = nc.dram_tensor("dbg_k", [128, 4096], F32, kind="ExternalOutput").ap()
        ctx["dbg_v"] = nc.dram_tensor("dbg_v", [128, 4096], F32, kind="ExternalOutput").ap()
        ctx["dbg_y"] = nc.dram_tensor("dbg_y", [128, 4096], F32, kind="ExternalOutput").ap()
        ctx["dbg_z"] = nc.dram_tensor("dbg_z", [128, 4096], F32, kind="ExternalOutput").ap()
    if DBG and 3 in blocks:
        ctx["dbg3_wfm"] = nc.dram_tensor("dbg3_wfm", [16, 512], F32, kind="ExternalOutput").ap()
        ctx["dbg3_wbc"] = nc.dram_tensor("dbg3_wbc", [128, 512], F32, kind="ExternalOutput").ap()
        ctx["dbg3_h"] = nc.dram_tensor("dbg3_h", [128, 16384], F32, kind="ExternalOutput").ap()
        ctx["dbg3_z"] = nc.dram_tensor("dbg3_z", [128, 4096], F32, kind="ExternalOutput").ap()
        ctx["dbg3_l"] = nc.dram_tensor("dbg3_l", [128, 64], F32, kind="ExternalOutput").ap()
    if DBG and 1 in blocks:
        ctx["dbg_wfm"] = nc.dram_tensor("dbg_wfm", [8, 512], F32, kind="ExternalOutput").ap()
        ctx["dbg_wbc"] = nc.dram_tensor("dbg_wbc", [128, 4096], F32, kind="ExternalOutput").ap()
        ctx["dbg_z1"] = nc.dram_tensor("dbg_z1", [128, 4096], F32, kind="ExternalOutput").ap()
        ctx["dbg_h1"] = nc.dram_tensor("dbg_h1", [128, 11264], F32, kind="ExternalOutput").ap()
    ctx["sb"] = sb
    ctx["ps"] = ps
    P = Plan(nc)
    emit_prologue(P, ctx)
    for b in blocks:
        EMITTERS[b](P, ctx)
    P.w("sync", ctx["g_out_final"])
    P.run()
    return nc




def emit_prologue(P, ctx):
    sb, ps, blocks = ctx["sb"], ctx["ps"], ctx["blocks"]
    Lc = P.sem("Lc")
    Vc = P.sem("Vc")
    Mc = P.sem("Mc")
    P.chain["vector"] = Vc

    P.i("vector", lambda e: e.memset(sb["ones1"][:], 1.0), inc=(Vc, 1))
    P.i("vector", lambda e: e.memset(sb["ones128"][:], 1.0), inc=(Vc, 1))
    P.i("vector", lambda e: e.memset(sb["eps5"][:], 1e-5), inc=(Vc, 1))
    P.i("vector", lambda e: e.memset(sb["eps8"][:], 1e-8), inc=(Vc, 1))
    P.i("gpsimd", lambda e: e.dma_start(out=sb["ident_bf"][:], in_=ctx["ident_d"]), inc=(Lc, 16))
    if 1 in blocks:
        P.i("sync", lambda e: e.dma_start(out=sb["routerT"][:].rearrange("p (kc e) -> p kc e", e=8), in_=ctx["routerT"].rearrange("(kc p) e -> p kc e", p=128)), inc=(Lc, 16))
        P.i("gpsimd", lambda e: e.dma_start(out=sb["sel8"][:], in_=ctx["sel8_d"]), inc=(Lc, 16))
    if 2 in blocks:
        P.i("sync", lambda e: e.dma_start(out=sb["mask"][:], in_=ctx["mask_d"]), inc=(Lc, 16))
        P.i("sync", lambda e: e.dma_start(out=sb["qgam"][:], in_=ctx["qgam_d"]), inc=(Lc, 16))
        P.i("sync", lambda e: e.dma_start(out=sb["vgam"][:], in_=ctx["vgam_d"]), inc=(Lc, 16))
        P.i("sync", lambda e: e.dma_start(out=sb["hsel"][:], in_=ctx["hsel_d"]), inc=(Lc, 16))
        P.i("sync", lambda e: e.dma_start(out=sb["bsel"][:], in_=ctx["bsel_d"]), inc=(Lc, 16))
    if 3 in blocks:
        P.i("sync", lambda e: e.dma_start(out=sb["keys_sb"][:].rearrange("p (kc e) -> p kc e", e=16), in_=ctx["keysT"].rearrange("(kc p) e -> p kc e", p=128)), inc=(Lc, 16))
        P.i("sync", lambda e: e.dma_start(out=sb["rw_sb"][:].rearrange("p (kc e) -> p kc e", e=16), in_=ctx["rwT"].rearrange("(kc p) e -> p kc e", p=128)), inc=(Lc, 16))
        P.i("sync", lambda e: e.dma_start(out=sb["rb_sb"][:], in_=ctx["rb_d"]), inc=(Lc, 16))
        P.i("sync", lambda e: e.dma_start(out=sb["b1_all"][:], in_=ctx["b1_d"]), inc=(Lc, 16))
        P.i("gpsimd", lambda e: e.dma_start(out=sb["b2_bf"][:], in_=ctx["b2_d"]), inc=(Lc, 16))
        P.i("gpsimd", lambda e: e.dma_start(out=sb["sel16"][:], in_=ctx["sel16_d"]), inc=(Lc, 16))
    ctx["g_const"] = (Lc, Lc.n)
    ctx["g_vconst"] = (Vc, Vc.n)

    if 3 in blocks:
        P.w("vector", ctx["g_const"])
        gv = None
        for kc in range(8):
            gv = P.i("vector", lambda e, kc=kc: e.scalar_tensor_tensor(
                sb["wcomb"][:, kc * 16:(kc + 1) * 16], sb["keys_sb"][:, kc * 16:(kc + 1) * 16], 2.0,
                sb["rw_sb"][:, kc * 16:(kc + 1) * 16], ALU.mult, ALU.add), inc=(Vc, 1))
        gsq = None
        for kc in range(8):
            gsq = P.i("vector", lambda e, kc=kc: e.tensor_mul(
                sb["ln_sq"][:, kc * 16:(kc + 1) * 16], sb["keys_sb"][:, kc * 16:(kc + 1) * 16],
                sb["keys_sb"][:, kc * 16:(kc + 1) * 16]), inc=(Vc, 1))
        P.w("tensor", gsq)
        P.w("tensor", ctx["g_vconst"])
        gm = None
        for kc in range(8):
            gm = P.i("tensor", lambda e, kc=kc: e.matmul(
                ps["psU"][0:1, 0:16], sb["ones128"][:], sb["ln_sq"][:, kc * 16:(kc + 1) * 16],
                start=(kc == 0), stop=(kc == 7)), inc=(Mc, 1) if kc == 7 else None)
        P.w("vector", gm)
        P.i("vector", lambda e: e.tensor_copy(sb["kkrow"][:], ps["psU"][0:1, 0:16]), inc=(Vc, 1))
        gcrow = P.i("vector", lambda e: e.tensor_sub(sb["crow"][:], sb["rb_sb"][:], sb["kkrow"][:]), inc=(Vc, 1))
        ctx["g_wcomb"] = gv
        ctx["g_crow"] = gcrow
        ctx["g_kk_read"] = gcrow
        ctx["VcSem"] = Vc




def emit_ln_store(P, ctx, out_stream, g, s2, sems, prev_store, load_gate=None):
    sb, ps = ctx["sb"], ctx["ps"]
    M, V, C, SO = sems["M"], sems["V"], sems["C"], sems["SO"]
    t0 = g * G + s2 * 256
    z = lambda kc: sb["y_sb"][:, kc * G + s2 * 256: kc * G + s2 * 256 + 256]
    if SKIP_LN:
        P.w("vector", prev_store)
        glast = None
        for kc in range(8):
            glast = P.i("vector", lambda e, kc=kc: e.tensor_copy(sb["ln_out"][:, kc * 256:(kc + 1) * 256], z(kc)), inc=(V, 1))
        P.w("sync", glast)
        dst = rearr(out_stream[:, t0:t0 + 256])
        return P.dma("sync", lambda e: e.dma_start(out=dst, in_=sb3(sb["ln_out"], 256)))
    sq = sb["ln_sq"][:, s2 * 256:(s2 + 1) * 256]
    row = lambda i: sb["ln_row"][0:1, i * 256:(i + 1) * 256]

    P.w("tensor", (C, C.n))
    P.w("tensor", (V, V.n))
    P.w("scalar", (V, V.n))
    P.w("scalar", load_gate)
    P.w("vector", load_gate)
    g_s1 = None
    g_s2 = None
    for kc in range(8):
        g_s1 = P.i("tensor", lambda e, kc=kc: e.matmul(
            ps["psU"][0:1, 0:256], sb["ones128"][:], z(kc), start=(kc == 0), stop=(kc == 7)), inc=(M, 1))
    for kc in range(8):
        gsq = P.i("scalar", lambda e, kc=kc: e.activation(sq, z(kc), AF.Square), inc=(C, 1))
        P.w("tensor", gsq)
        g_s2 = P.i("tensor", lambda e, kc=kc: e.matmul(
            ps["psU"][0:1, 256:512], sb["ones128"][:], sq, start=(kc == 0), stop=(kc == 7)), inc=(M, 1))
        if kc < 7:
            P.w("scalar", g_s2)
    P.w("vector", g_s2)
    P.i("vector", lambda e: e.tensor_scalar_mul(row(0), ps["psU"][0:1, 0:256], 1.0 / 1024.0), inc=(V, 1))
    P.i("vector", lambda e: e.tensor_scalar_mul(row(1), ps["psU"][0:1, 256:512], 1.0 / 1024.0), inc=(V, 1))
    P.i("vector", lambda e: e.tensor_mul(row(2), row(0), row(0)), inc=(V, 1))
    gvar = P.i("vector", lambda e: e.tensor_sub(row(1), row(1), row(2)), inc=(V, 1))
    if "var_dump" in ctx:
        st_i = g * 2 + s2
        P.w("sync", gvar)
        gvd = P.i("sync", lambda e, st_i=st_i: e.dma_start(
            out=ctx["var_dump"][:, st_i * 256:(st_i + 1) * 256], in_=sb["ln_row"][0:1, 256:512]), inc=(SO, 16))
        P.w("scalar", gvd)
    P.w("scalar", gvar)
    gsd = P.i("scalar", lambda e: e.activation(row(1), row(1), AF.Sqrt, bias=sb["eps5"][0:1, 0:1]), inc=(C, 1))
    P.w("vector", gsd)
    P.i("vector", lambda e: e.reciprocal(row(2), row(1)), inc=(V, 1))
    gnm = P.i("vector", lambda e: e.scalar_tensor_tensor(row(3), row(0), -1.0, row(2), ALU.mult, ALU.mult), inc=(V, 1))
    P.w("tensor", gnm)
    gb1 = P.i("tensor", lambda e: e.matmul(ps["psC"][:, 0:256], sb["ones1"][:], row(2), start=True, stop=True), inc=(M, 1))
    gb2 = P.i("tensor", lambda e: e.matmul(ps["psD"][:, 0:256], sb["ones1"][:], row(3), start=True, stop=True), inc=(M, 1))
    P.w("scalar", gb1)
    gc1 = P.i("scalar", lambda e: e.copy(sb["ln_bc1"][:], ps["psC"][:, 0:256]), inc=(C, 1))
    P.w("scalar", gb2)
    gc2 = P.i("scalar", lambda e: e.copy(sb["ln_bc2"][:], ps["psD"][:, 0:256]), inc=(C, 1))
    P.w("vector", gc1)
    P.w("vector", gc2)
    P.w("vector", prev_store)
    glast = None
    for kc in range(8):
        ob = sb["ln_out"][:, kc * 256:(kc + 1) * 256]
        P.i("vector", lambda e, kc=kc, ob=ob: e.tensor_mul(ob, z(kc), sb["ln_bc1"][:]), inc=(V, 1))
        glast = P.i("vector", lambda e, kc=kc, ob=ob: e.tensor_add(ob, ob, sb["ln_bc2"][:]), inc=(V, 1))
    P.w("sync", glast)
    dst = rearr(out_stream[:, t0:t0 + 256])
    return P.dma("sync", lambda e: e.dma_start(out=dst, in_=sb3(sb["ln_out"], 256)))




def emit_l0_attn(P, ctx):
    sb, ps = ctx["sb"], ctx["ps"]
    sin, sout = ctx["in0"], ctx["out0"]
    Lg, Ls = P.sem("aLg"), P.sem("aLs")
    M, V, C, SO = P.sem("aM"), P.sem("aV"), P.sem("aC"), P.sem("aSO")
    P.chain["vector"] = V
    sems = dict(M=M, V=V, C=C, SO=SO)

    g_w_list = []
    for kc in range(8):
        g_w_list.append(P.dma("gpsimd", lambda e, kc=kc: e.dma_start(
            out=sb["a_wqkv"][:, kc * 3072:(kc + 1) * 3072],
            in_=ctx["wqkvT"][kc * 128:(kc + 1) * 128, :])))
        g_w_list.append(P.dma("gpsimd", lambda e, kc=kc: e.dma_start(
            out=sb["a_wproj"][:, kc * 1024:(kc + 1) * 1024],
            in_=ctx["wprojT"][kc * 128:(kc + 1) * 128, :])))
    g_w = g_w_list

    free_xb = None
    prev_store = None
    xb = lambda kc: sb["x_bf"][:, kc * G:(kc + 1) * G]

    for g in range(NG):
        t0 = g * G
        g_prev_M = (M, M.n)
        P.w("gpsimd", free_xb)
        gxb = P.dma("gpsimd", lambda e, t0=t0: e.dma_start(
            out=sb3(sb["x_bf"], G), in_=rearr(sin[:, t0:t0 + G])))
        P.w("sync", (V, V.n))
        gxf = P.dma("sync", lambda e, t0=t0: e.dma_start(
            out=sb3(sb["y_sb"], G), in_=rearr(sin[:, t0:t0 + G])))
        P.w("scalar", g_prev_M)
        P.w("vector", g_prev_M)

        for _g in g_w:
            P.w("tensor", _g)
        P.w("tensor", gxb)
        P.w("tensor", ctx["g_vconst"])
        P.w("tensor", ctx["g_const"])
        for qk in range(2):
            dstbuf = sb["a_q"] if qk == 0 else sb["a_k"]
            for oc in range(8):
                bank = ps["psE"] if oc % 2 == 0 else ps["psF"]
                gm = None
                for kc in range(8):
                    off = kc * 3072 + qk * 1024 + oc * 128
                    gm = P.i("tensor", lambda e, off=off, bank=bank, kc=kc: e.matmul(
                        bank[:], sb["a_wqkv"][:, off:off + 128], xb(kc),
                        start=(kc == 0), stop=(kc == 7)), inc=(M, 1) if kc == 7 else None)
                P.w("scalar", gm)
                ge = P.i("scalar", lambda e, oc=oc, bank=bank, dstbuf=dstbuf: e.copy(
                    dstbuf[:, oc * G:(oc + 1) * G], bank[:]), inc=(C, 1))
                P.w("tensor", ge)
        for tt in range(4):
            for hf in range(2):
                bank = ps["psA"] if hf == 0 else ps["psB"]
                gm = None
                for kc in range(8):
                    off = kc * 3072 + 2048 + hf * 512
                    gm = P.i("tensor", lambda e, kc=kc, tt=tt, off=off, bank=bank: e.matmul(
                        bank[:], xb(kc)[:, tt * 128:(tt + 1) * 128], sb["a_wqkv"][:, off:off + 512],
                        start=(kc == 0), stop=(kc == 7)), inc=(M, 1) if kc == 7 else None)
                P.w("vector", gm)
                ge = P.i("vector", lambda e, tt=tt, hf=hf, bank=bank: e.tensor_copy(
                    sb["a_v"][:, tt * 1024 + hf * 512: tt * 1024 + (hf + 1) * 512], bank[:]), inc=(V, 1))
                P.w("tensor", ge)

        last_gpn = None
        last_gt = None
        last_gpv = None
        for w in range(2):
            wt = w * 256
            for h in range(NHEAD):
                ch, hb = h // 2, (h % 2) * 64
                qsl = sb["a_q"][hb:hb + 64, ch * G + wt: ch * G + wt + 256]
                ksl = sb["a_k"][hb:hb + 64, ch * G + wt: ch * G + wt + 256]
                for it in range(2):
                    gs = P.i("tensor", lambda e, qsl=qsl, ksl=ksl, it=it: e.matmul(
                        ps["psC"][:, 0:256], qsl[:, it * 128:(it + 1) * 128], ksl,
                        start=True, stop=True), inc=(M, 1))
                    P.w("vector", gs)
                    P.i("vector", lambda e: e.reduce_max(sb["col_a"][:], ps["psC"][:, 0:256], axis=AX.X), inc=(V, 1))
                    gng = P.i("vector", lambda e: e.tensor_scalar_mul(sb["col_b"][:], sb["col_a"][:], -0.125), inc=(V, 1))
                    P.w("scalar", gng)
                    P.w("scalar", last_gpn)
                    gexp = P.i("scalar", lambda e: e.activation(
                        sb["a_e"][:], ps["psC"][:, 0:256], AF.Exp, bias=sb["col_b"][:, 0:1], scale=0.125), inc=(C, 1))
                    P.w("tensor", gexp)
                    P.w("vector", gexp)
                    P.i("vector", lambda e: e.reduce_sum(sb["col_c"][:], sb["a_e"][:], axis=AX.X), inc=(V, 1))
                    P.i("vector", lambda e: e.reciprocal(sb["col_d"][:], sb["col_c"][:]), inc=(V, 1))
                    P.w("vector", last_gt)
                    gpn = P.i("vector", lambda e: e.tensor_scalar(
                        sb["a_p"][:], sb["a_e"][:], sb["col_d"][:, 0:1], None, ALU.mult), inc=(V, 1))
                    last_gpn = gpn
                    P.w("tensor", gpn)
                    P.w("vector", last_gpv)
                    for jt in range(2):
                        gt = P.i("tensor", lambda e, jt=jt: e.transpose(
                            ps["psT"][:], sb["a_p"][:, jt * 128:(jt + 1) * 128], sb["ident_bf"][:]), inc=(M, 1))
                        last_gt = gt
                        P.w("vector", gt)
                        gpt = P.i("vector", lambda e, jt=jt: e.tensor_copy(
                            sb["a_pt"][:, jt * 128:(jt + 1) * 128], ps["psT"][:]), inc=(V, 1))
                        P.w("tensor", gpt)
                    gpv = None
                    for jt in range(2):
                        voff = (w * 2 + jt) * 1024 + h * 64
                        gpv = P.i("tensor", lambda e, jt=jt, voff=voff: e.matmul(
                            ps["psD"][0:64, 0:128], sb["a_v"][:, voff:voff + 64],
                            sb["a_pt"][:, jt * 128:(jt + 1) * 128],
                            start=(jt == 0), stop=(jt == 1)), inc=(M, 1) if jt == 1 else None)
                    last_gpv = gpv
                    P.w("scalar", gpv)
                    gye = P.i("scalar", lambda e, ch=ch, hb=hb, wt=wt, it=it: e.copy(
                        sb["a_y"][hb:hb + 64, ch * G + wt + it * 128: ch * G + wt + (it + 1) * 128],
                        ps["psD"][0:64, 0:128]), inc=(C, 1))
                    P.w("tensor", gye)

        if DBG and g == 0:
            gdb = (C, C.n)
            P.w("gpsimd", gdb)
            P.w("gpsimd", (V, V.n))
            P.i("gpsimd", lambda e: e.dma_start(out=ctx["dbg_q"], in_=sb["a_q"][:]), inc=(SO, 16))
            P.i("gpsimd", lambda e: e.dma_start(out=ctx["dbg_k"], in_=sb["a_k"][:]), inc=(SO, 16))
            P.i("gpsimd", lambda e: e.dma_start(out=ctx["dbg_v"], in_=sb["a_v"][:]), inc=(SO, 16))
            P.i("gpsimd", lambda e: e.dma_start(out=ctx["dbg_y"], in_=sb["a_y"][:]), inc=(SO, 16))
        P.w("tensor", (C, C.n))
        P.w("tensor", gxf)
        for oc in range(8):
            bank = ps["psE"] if oc % 2 == 0 else ps["psF"]
            gm = None
            for kc in range(8):
                off = kc * 1024 + oc * 128
                gm = P.i("tensor", lambda e, kc=kc, off=off, bank=bank: e.matmul(
                    bank[:], sb["a_wproj"][:, off:off + 128], sb["a_y"][:, kc * G:(kc + 1) * G],
                    start=(kc == 0), stop=(kc == 7)), inc=(M, 1) if kc == 7 else None)
            P.w("vector", gm)
            gz = P.i("vector", lambda e, oc=oc, bank=bank: e.tensor_add(
                sb["y_sb"][:, oc * G:(oc + 1) * G], sb["y_sb"][:, oc * G:(oc + 1) * G], bank[:]), inc=(V, 1))
            P.w("tensor", gz)
        free_xb = (M, M.n)
        if DBG and g == 0:
            P.w("sync", (V, V.n))
            P.i("sync", lambda e: e.dma_start(out=ctx["dbg_z"], in_=sb["y_sb"][:]), inc=(SO, 16))

        for s2 in range(2):
            prev_store = emit_ln_store(P, ctx, sout, g, s2, sems, prev_store, gxf)

    ctx["g_out_final"] = prev_store




def emit_l0_ffn(P, ctx):
    sb, ps = ctx["sb"], ctx["ps"]
    sin, sout = ctx["in1"], ctx["out1"]
    Lg, Ls = P.sem("fLg"), P.sem("fLs")
    M, V, C, SO = P.sem("fM"), P.sem("fV"), P.sem("fC"), P.sem("fSO")
    P.chain["vector"] = V
    sems = dict(M=M, V=V, C=C, SO=SO)

    free_xb = None
    prev_store = None
    g_p1_done = None
    g_p2_done = None
    g_p2_vec_done = None
    xb = lambda kc: sb["x_bf"][:, kc * G:(kc + 1) * G]
    yv = lambda oc: sb["y_sb"][:, oc * G:(oc + 1) * G]

    for g in range(NG):
        t0 = g * G
        g_prev_M = (M, M.n)
        g_prev_V = (V, V.n)
        P.w("gpsimd", free_xb)
        gxb = P.dma("gpsimd", lambda e, t0=t0: e.dma_start(
            out=sb3(sb["x_bf"], G), in_=rearr(sin[:, t0:t0 + G])))
        P.w("sync", (V, V.n))
        gxf = P.dma("sync", lambda e, t0=t0: e.dma_start(
            out=sb3(sb["y_sb"], G), in_=rearr(sin[:, t0:t0 + G])))
        P.w("scalar", g_prev_M)
        P.w("vector", g_prev_M)

        P.w("tensor", ctx["g_const"])
        P.w("tensor", ctx["g_vconst"])
        P.w("tensor", gxf)
        P.w("tensor", gxb)
        P.w("tensor", g_prev_V)
        last_gexp = None
        last_gtr = None
        for tt in range(4):
            if tt > 0:
                P.w("tensor", (V, V.n))
            gl = None
            for kc in range(8):
                gl = P.i("tensor", lambda e, kc=kc, tt=tt: e.matmul(
                    ps["psU"][:, tt * 8:(tt + 1) * 8], yv(kc)[:, tt * 128:(tt + 1) * 128],
                    sb["routerT"][:, kc * 8:(kc + 1) * 8],
                    start=(kc == 0), stop=(kc == 7)), inc=(M, 1) if kc == 7 else None)
            P.w("vector", gl)
            P.w("vector", last_gexp)
            P.i("vector", lambda e, tt=tt: e.tensor_copy(sb["p8_a"][:, 0:8], ps["psU"][:, tt * 8:(tt + 1) * 8]), inc=(V, 1))
            P.i("vector", lambda e: e.reduce_max(sb["col_a"][:], sb["p8_a"][:, 0:8], axis=AX.X), inc=(V, 1))
            gng = P.i("vector", lambda e: e.tensor_scalar_mul(sb["col_b"][:], sb["col_a"][:], -1.0), inc=(V, 1))
            P.w("scalar", gng)
            gexp = P.i("scalar", lambda e: e.activation(
                sb["p8_b"][:, 0:8], sb["p8_a"][:, 0:8], AF.Exp, bias=sb["col_b"][:, 0:1]), inc=(C, 1))
            last_gexp = gexp
            P.w("vector", gexp)
            P.i("vector", lambda e: e.reduce_sum(sb["col_c"][:], sb["p8_b"][:, 0:8], axis=AX.X), inc=(V, 1))
            P.i("vector", lambda e: e.reciprocal(sb["col_d"][:], sb["col_c"][:]), inc=(V, 1))
            P.i("vector", lambda e: e.tensor_scalar(
                sb["p8_a"][:, 0:8], sb["p8_b"][:, 0:8], sb["col_d"][:, 0:1], None, ALU.mult), inc=(V, 1))
            P.i("vector", lambda e: e.reduce_max(sb["col_a"][:], sb["p8_a"][:, 0:8], axis=AX.X), inc=(V, 1))
            P.i("vector", lambda e: e.tensor_scalar(
                sb["p8_c"][:, 0:8], sb["p8_a"][:, 0:8], sb["col_a"][:, 0:1], None, ALU.is_ge), inc=(V, 1))
            P.i("vector", lambda e: e.scalar_tensor_tensor(
                sb["p8_b"][:, 0:8], sb["p8_c"][:, 0:8], -1e30, sb["p8_a"][:, 0:8], ALU.mult, ALU.add), inc=(V, 1))
            P.i("vector", lambda e: e.reduce_max(sb["col_b"][:], sb["p8_b"][:, 0:8], axis=AX.X), inc=(V, 1))
            P.i("vector", lambda e: e.tensor_scalar(
                sb["p8_c"][:, 0:8], sb["p8_a"][:, 0:8], sb["col_b"][:, 0:1], None, ALU.is_ge), inc=(V, 1))
            P.w("vector", last_gtr)
            gwt = P.i("vector", lambda e: e.tensor_mul(sb["wte_bf"][:, 0:8], sb["p8_a"][:, 0:8], sb["p8_c"][:, 0:8]), inc=(V, 1))
            P.w("tensor", gwt)
            gtr = P.i("tensor", lambda e: e.transpose(ps["psT"][0:8, 0:128], sb["wte_bf"][:, 0:8], sb["ident_bf"][:]), inc=(M, 1))
            last_gtr = gtr
            P.w("vector", gtr)
            gwf = P.i("vector", lambda e, tt=tt: e.tensor_copy(
                sb["w_fm8"][:, tt * 128:(tt + 1) * 128], ps["psT"][0:8, 0:128]), inc=(V, 1))
            P.w("tensor", gwf)

        gwfm = (V, V.n)
        P.w("tensor", gwfm)
        P.w("scalar", g_p2_vec_done)
        for e8 in range(8):
            bank = ps["psU"] if e8 % 2 == 0 else ps["psC"]
            gb = P.i("tensor", lambda e, e8=e8, bank=bank: e.matmul(
                bank[:], sb["sel8"][:, e8 * 128:(e8 + 1) * 128], sb["w_fm8"][:],
                start=True, stop=True), inc=(M, 1))
            P.w("scalar", gb)
            gwbc = P.i("scalar", lambda e, e8=e8, bank=bank: e.copy(
                sb["f_wbc"][:, e8 * G:(e8 + 1) * G], bank[:]), inc=(C, 1))
            P.w("tensor", gwbc)
        g_wbc_all = (C, C.n)
        if DBG and g == 0:
            P.w("gpsimd", g_wbc_all)
            P.w("gpsimd", gwfm)
            P.dma("gpsimd", lambda e: e.dma_start(out=ctx["dbg_wfm"], in_=sb["w_fm8"][:]))
            P.dma("gpsimd", lambda e: e.dma_start(out=ctx["dbg_wbc"], in_=sb["f_wbc"][:]))

        for e8 in range(8):
            P.w("gpsimd", g_p1_done)
            P.w("gpsimd", (V, V.n))
            P.w("gpsimd", (C, C.n))
            gld_gate = []
            for kc in range(8):
                gg2 = P.dma("gpsimd", lambda e, kc=kc, e8=e8: e.dma_start(
                    out=sb["f_gate"][:, kc * 2816:(kc + 1) * 2816],
                    in_=ctx["gateT"][e8, kc * 128:(kc + 1) * 128, :]))
                gld_gate.append(gg2)
            P.w("gpsimd", g_p2_done)
            gld_up = []
            for kc in range(8):
                gu = P.dma("gpsimd", lambda e, kc=kc, e8=e8: e.dma_start(
                    out=sb["f_up"][:, kc * 2816:(kc + 1) * 2816],
                    in_=ctx["upT"][e8, kc * 128:(kc + 1) * 128, :]))
                gld_up.append(gu)

            P.w("tensor", g_wbc_all)
            P.w("vector", g_p2_done)
            last_gsi = {0: None, 1: None}
            last_gt1 = {0: None, 1: None}
            for ic in range(22):
                par = ic % 2
                bg = ps["psE"] if par == 0 else ps["psA"]
                bu = ps["psF"] if par == 0 else ps["psB"]
                P.w("tensor", last_gsi[par])
                P.w("tensor", last_gt1[par])
                gm = None
                for kc in range(8):
                    if ic == 0:
                        P.w("tensor", gld_gate[kc])
                    gm = P.i("tensor", lambda e, kc=kc, ic=ic, bg=bg: e.matmul(
                        bg[:], sb["f_gate"][:, kc * 2816 + ic * 128: kc * 2816 + (ic + 1) * 128], xb(kc),
                        start=(kc == 0), stop=(kc == 7)), inc=(M, 1) if kc == 7 else None)
                gmu = None
                for kc in range(8):
                    if ic == 0:
                        P.w("tensor", gld_up[kc])
                    gmu = P.i("tensor", lambda e, kc=kc, ic=ic, bu=bu: e.matmul(
                        bu[:], sb["f_up"][:, kc * 2816 + ic * 128: kc * 2816 + (ic + 1) * 128], xb(kc),
                        start=(kc == 0), stop=(kc == 7)), inc=(M, 1) if kc == 7 else None)
                P.w("scalar", gm)
                P.w("scalar", last_gt1[par])
                gsi = P.i("scalar", lambda e, par=par, bg=bg: e.activation(
                    sb["f_sg"][:, par * 512:(par + 1) * 512], bg[:], AF.Silu), inc=(C, 1))
                last_gsi[par] = gsi
                P.w("vector", gsi)
                P.w("vector", gmu)
                gt1 = P.i("vector", lambda e, par=par, bu=bu: e.tensor_mul(
                    sb["f_t"][:, par * 512:(par + 1) * 512],
                    sb["f_sg"][:, par * 512:(par + 1) * 512], bu[:]), inc=(V, 1))
                last_gt1[par] = gt1
                P.i("vector", lambda e, ic=ic, par=par, e8=e8: e.tensor_mul(
                    sb["f_h"][:, ic * 512:(ic + 1) * 512],
                    sb["f_t"][:, par * 512:(par + 1) * 512],
                    sb["f_wbc"][:, e8 * G:(e8 + 1) * G]), inc=(V, 1))
            g_p1_done = (M, M.n)
            if DBG and g == 0 and e8 == 0:
                P.w("gpsimd", (V, V.n))
                P.dma("gpsimd", lambda e: e.dma_start(out=ctx["dbg_h1"], in_=sb["f_h"][:]))

            P.w("gpsimd", g_p1_done)
            P.w("gpsimd", (V, V.n))
            P.w("gpsimd", (C, C.n))
            gld_down = []
            for ic in range(22):
                gd = P.dma("gpsimd", lambda e, ic=ic, e8=e8: e.dma_start(
                    out=sb["f_down"][:, ic * 1024:(ic + 1) * 1024],
                    in_=ctx["downT"][e8, ic * 128:(ic + 1) * 128, :]))
                gld_down.append(gd)

            h_done = (V, V.n)
            P.w("tensor", h_done)
            for r in range(2):
                banks = [ps["psA"], ps["psB"], ps["psC"], ps["psD"]]
                gstop = [None] * 4
                for ic in range(22):
                    if r == 0:
                        P.w("tensor", gld_down[ic])
                    for j in range(4):
                        dc = r * 4 + j
                        gstop[j] = P.i("tensor", lambda e, ic=ic, dc=dc, bank=banks[j]: e.matmul(
                            bank[:], sb["f_down"][:, ic * 1024 + dc * 128: ic * 1024 + (dc + 1) * 128],
                            sb["f_h"][:, ic * 512:(ic + 1) * 512],
                            start=(ic == 0), stop=(ic == 21)), inc=(M, 1) if ic == 21 else None)
                gev = None
                for j in range(4):
                    dc = r * 4 + j
                    P.w("vector", gstop[j])
                    gev = P.i("vector", lambda e, dc=dc, bank=banks[j]: e.tensor_add(
                        yv(dc), yv(dc), bank[:]), inc=(V, 1))
                P.w("tensor", gev)
            g_p2_done = (M, M.n)
            g_p2_vec_done = (V, V.n)

        free_xb = (M, M.n)
        if DBG and g == 0:
            P.w("gpsimd", (V, V.n))
            P.dma("gpsimd", lambda e: e.dma_start(out=ctx["dbg_z1"], in_=sb["y_sb"][:]))
        for s2 in range(2):
            prev_store = emit_ln_store(P, ctx, sout, g, s2, sems, prev_store, gxf)

    ctx["g_out_final"] = prev_store




def emit_l1_attn(P, ctx):
    sb, ps = ctx["sb"], ctx["ps"]
    sin, sout = ctx["in2"], ctx["out2"]
    Lg, Ls = P.sem("lLg"), P.sem("lLs")
    M, V, C, SO = P.sem("lM"), P.sem("lV"), P.sem("lC"), P.sem("lSO")
    P.chain["vector"] = V
    sems = dict(M=M, V=V, C=C, SO=SO)

    g_w_list = []
    for kc in range(8):
        g_w_list.append(P.dma("gpsimd", lambda e, kc=kc: e.dma_start(
            out=sb["la_wq"][:, kc * 1024:(kc + 1) * 1024],
            in_=ctx["wqT"][kc * 128:(kc + 1) * 128, :])))
        g_w_list.append(P.dma("gpsimd", lambda e, kc=kc: e.dma_start(
            out=sb["la_wv"][:, kc * 1024:(kc + 1) * 1024],
            in_=ctx["wvT"][kc * 128:(kc + 1) * 128, :])))
    g_w = g_w_list

    free_xb = None
    prev_store = None
    xb = lambda kc: sb["x_bf"][:, kc * G:(kc + 1) * G]

    for s in range(2):
        P.w("vector", (M, M.n))
        P.w("vector", ctx.get("g_smirror"))
        P.i("vector", lambda e: e.memset(sb["S_sb"][:], 0.0), inc=(V, 1))
        P.i("vector", lambda e: e.memset(sb["S_bf"][:], 0.0), inc=(V, 1))
        g_state = (V, V.n)
        ctx["g_smirror"] = None
        for gg in range(4):
            g = s * 4 + gg
            t0 = g * G
            g_prev_M = (M, M.n)
            P.w("gpsimd", free_xb)
            gxb = P.dma("gpsimd", lambda e, t0=t0: e.dma_start(
                out=sb3(sb["x_bf"], G), in_=rearr(sin[:, t0:t0 + G])))
            P.w("sync", (V, V.n))
            gxf = P.dma("sync", lambda e, t0=t0: e.dma_start(
                out=sb3(sb["y_sb"], G), in_=rearr(sin[:, t0:t0 + G])))
            P.w("scalar", g_prev_M)
            P.w("vector", g_prev_M)

            for _g in g_w:
                P.w("tensor", _g)
            P.w("tensor", gxb)
            P.w("tensor", ctx["g_const"])
            P.w("tensor", ctx["g_vconst"])
            for oc in range(8):
                bank = ps["psE"] if oc % 2 == 0 else ps["psF"]
                gm = None
                for kc in range(8):
                    off = kc * 1024 + oc * 128
                    gm = P.i("tensor", lambda e, kc=kc, off=off, bank=bank: e.matmul(
                        bank[:], sb["la_wq"][:, off:off + 128], xb(kc),
                        start=(kc == 0), stop=(kc == 7)), inc=(M, 1) if kc == 7 else None)
                P.w("scalar", gm)
                ge = P.i("scalar", lambda e, oc=oc, bank=bank: e.copy(
                    sb["la_qpre"][:, oc * G:(oc + 1) * G], bank[:]), inc=(C, 1))
                P.w("tensor", ge)
            P.w("vector", (C, C.n))
            P.w("vector", ctx["g_const"])
            g_st = None
            for kc in range(8):
                gv = P.i("vector", lambda e, kc=kc: e.tensor_mul(
                    sb["ln_sq"][:], sb["la_qpre"][:, kc * G:(kc + 1) * G],
                    sb["la_qpre"][:, kc * G:(kc + 1) * G]), inc=(V, 1))
                P.w("tensor", gv)
                g_st = P.i("tensor", lambda e, kc=kc: e.matmul(
                    ps["psU"][0:16, 0:512], sb["hsel"][:, kc * 16:(kc + 1) * 16], sb["ln_sq"][:],
                    start=(kc == 0), stop=(kc == 7)), inc=(M, 1))
                if kc < 7:
                    P.w("vector", g_st)
            P.w("vector", g_st)
            P.i("vector", lambda e: e.tensor_scalar_mul(
                sb["la_s16"][0:16, 512:1024], ps["psU"][0:16, 0:512], 1.0 / 64.0), inc=(V, 1))
            P.w("scalar", (V, V.n))
            gsd = P.i("scalar", lambda e: e.activation(
                sb["la_s16"][0:16, 512:1024], sb["la_s16"][0:16, 512:1024], AF.Sqrt, bias=sb["eps8"][0:16, 0:1]), inc=(C, 1))
            P.w("vector", gsd)
            grs = P.i("vector", lambda e: e.reciprocal(
                sb["la_s16"][0:16, 0:512], sb["la_s16"][0:16, 512:1024]), inc=(V, 1))
            P.w("tensor", grs)
            last_gqn = None
            for kc in range(8):
                bank = ps["psU"] if kc % 2 == 0 else ps["psC"]
                gb = P.i("tensor", lambda e, kc=kc, bank=bank: e.matmul(
                    bank[:], sb["bsel"][:, kc * 128:(kc + 1) * 128], sb["la_s16"][0:16, 0:512],
                    start=True, stop=True), inc=(M, 1))
                P.w("scalar", gb)
                P.w("scalar", last_gqn)
                gbc = P.i("scalar", lambda e, kc=kc, bank=bank: e.activation(
                    sb["ln_out"][:, 0:512], bank[:], AF.Copy, scale=sb["qgam"][:, kc:kc + 1]), inc=(C, 1))
                P.w("vector", gbc)
                gqn = P.i("vector", lambda e, kc=kc: e.tensor_mul(
                    sb["la_qn"][:, kc * G:(kc + 1) * G], sb["la_qpre"][:, kc * G:(kc + 1) * G],
                    sb["ln_out"][:, 0:512]), inc=(V, 1))
                last_gqn = gqn
                P.w("tensor", gqn)

            for tt in range(4):
                for hf in range(2):
                    bank = ps["psA"] if hf == 0 else ps["psB"]
                    gm = None
                    for kc in range(8):
                        off = kc * 1024 + hf * 512
                        gm = P.i("tensor", lambda e, kc=kc, tt=tt, off=off, bank=bank: e.matmul(
                            bank[:], xb(kc)[:, tt * 128:(tt + 1) * 128], sb["la_wv"][:, off:off + 512],
                            start=(kc == 0), stop=(kc == 7)), inc=(M, 1) if kc == 7 else None)
                    P.w("vector", gm)
                    gvf = P.i("vector", lambda e, hf=hf, bank=bank: e.tensor_copy(
                        sb["la_vf"][:, hf * 512:(hf + 1) * 512], bank[:]), inc=(V, 1))
                    P.w("tensor", gvf)
                P.i("vector", lambda e: e.tensor_mul(sb["ln_sq"][:], sb["la_vf"][:, 0:512], sb["la_vf"][:, 0:512]), inc=(V, 1))
                P.i("vector", lambda e: e.tensor_mul(sb["ln_out"][:, 0:512], sb["la_vf"][:, 512:1024], sb["la_vf"][:, 512:1024]), inc=(V, 1))
                for h in range(16):
                    src = sb["ln_sq"] if h < 8 else sb["ln_out"][:, 0:512]
                    off = (h % 8) * 64
                    P.i("vector", lambda e, h=h, src=src, off=off: e.reduce_sum(
                        sb["la_vs"][:, h:h + 1], src[:, off:off + 64], axis=AX.X), inc=(V, 1))
                P.i("vector", lambda e: e.tensor_scalar_mul(sb["la_vs"][:, 16:32], sb["la_vs"][:, 0:16], 1.0 / 64.0), inc=(V, 1))
                P.w("scalar", (V, V.n))
                gvsd = P.i("scalar", lambda e: e.activation(
                    sb["la_vs"][:, 16:32], sb["la_vs"][:, 16:32], AF.Sqrt, bias=sb["eps8"][:, 0:1]), inc=(C, 1))
                P.w("vector", gvsd)
                P.i("vector", lambda e: e.reciprocal(sb["la_vs"][:, 0:16], sb["la_vs"][:, 16:32]), inc=(V, 1))
                for h in range(16):
                    P.i("vector", lambda e, h=h: e.tensor_scalar(
                        sb["la_vf"][:, h * 64:(h + 1) * 64],
                        sb["la_vf"][:, h * 64:(h + 1) * 64], sb["la_vs"][:, h:h + 1], None, ALU.mult), inc=(V, 1))
                gvn = P.i("vector", lambda e, tt=tt: e.tensor_mul(
                    sb["la_v"][:, tt * 1024:(tt + 1) * 1024], sb["la_vf"][:], sb["vgam"][:]), inc=(V, 1))
                P.w("tensor", gvn)

            for tt in range(4):
                for kc in range(8):
                    gt = P.i("tensor", lambda e, tt=tt, kc=kc: e.transpose(
                        ps["psT"][:], xb(kc)[:, tt * 128:(tt + 1) * 128], sb["ident_bf"][:]), inc=(M, 1))
                    P.w("vector", gt)
                    gx = P.i("vector", lambda e, tt=tt, kc=kc: e.tensor_copy(
                        sb["la_xtm"][:, tt * 1024 + kc * 128: tt * 1024 + (kc + 1) * 128], ps["psT"][:]), inc=(V, 1))
                    P.w("tensor", gx)

            P.w("tensor", gxf)
            P.w("tensor", g_state)
            last_gp = None
            last_gpt = None
            for c in range(4):
                g_supd_prev = (V, V.n)
                P.w("tensor", g_supd_prev)
                P.w("tensor", ctx.get("g_smirror"))
                for h in range(NHEAD):
                    ch, hb = h // 2, (h % 2) * 64
                    qn = sb["la_qn"][hb:hb + 64, ch * G + c * 128: ch * G + (c + 1) * 128]
                    first = (gg == 0 and c == 0)
                    gs = P.i("tensor", lambda e, qn=qn, ch=ch, hb=hb, c=c: e.matmul(
                        ps["psC"][:, 0:128], qn, xb(ch)[hb:hb + 64, c * 128:(c + 1) * 128],
                        start=True, stop=True), inc=(M, 1))
                    P.w("vector", gs)
                    P.w("vector", last_gp)
                    gp = P.i("vector", lambda e: e.tensor_mul(sb["la_p"][:], ps["psC"][:, 0:128], sb["mask"][:]), inc=(V, 1))
                    P.w("tensor", gp)
                    gt = P.i("tensor", lambda e: e.transpose(ps["psT"][:], sb["la_p"][:], sb["ident_bf"][:]), inc=(M, 1))
                    last_gp = gt
                    P.w("vector", gt)
                    P.w("vector", last_gpt)
                    gpt = P.i("vector", lambda e: e.tensor_copy(sb["la_pt"][:], ps["psT"][:]), inc=(V, 1))
                    P.w("tensor", gpt)
                    if not first:
                        P.i("tensor", lambda e, h=h, qn=qn, hb=hb: e.matmul(
                            ps["psD"][0:64, 0:128], sb["S_bf"][hb:hb + 64, h * 64:(h + 1) * 64], qn,
                            start=True, stop=False))
                    gpv = P.i("tensor", lambda e, c=c, h=h, first=first: e.matmul(
                        ps["psD"][0:64, 0:128], sb["la_v"][:, c * 1024 + h * 64: c * 1024 + h * 64 + 64],
                        sb["la_pt"][:], start=first, stop=True), inc=(M, 1))
                    last_gpt = gpv
                    P.w("vector", gpv)
                    gye = P.i("vector", lambda e, ch=ch, hb=hb, c=c: e.tensor_add(
                        sb["y_sb"][hb:hb + 64, ch * G + c * 128: ch * G + (c + 1) * 128],
                        sb["y_sb"][hb:hb + 64, ch * G + c * 128: ch * G + (c + 1) * 128],
                        ps["psD"][0:64, 0:128]), inc=(V, 1))
                    P.w("tensor", gye)
                    gsu = P.i("tensor", lambda e, c=c, h=h, ch=ch, hb=hb: e.matmul(
                        ps["psF"][0:64, 0:64],
                        sb["la_xtm"][:, c * 1024 + ch * 128 + hb: c * 1024 + ch * 128 + hb + 64],
                        sb["la_v"][:, c * 1024 + h * 64: c * 1024 + h * 64 + 64],
                        start=True, stop=True), inc=(M, 1))
                    P.w("vector", gsu)
                    P.i("vector", lambda e, h=h: e.tensor_add(
                        sb["S_sb"][:, h * 64:(h + 1) * 64], sb["S_sb"][:, h * 64:(h + 1) * 64],
                        ps["psF"][0:64, 0:64]), inc=(V, 1))
                    gsb = P.i("vector", lambda e, h=h: e.tensor_copy(
                        sb["S_bf"][0:64, h * 64:(h + 1) * 64], sb["S_sb"][:, h * 64:(h + 1) * 64]), inc=(V, 1))
                    P.w("tensor", gsb)
                P.w("gpsimd", (V, V.n))
                ctx["g_smirror"] = P.dma("gpsimd", lambda e: e.dma_start(
                    out=sb["S_bf"][64:128, :], in_=sb["S_bf"][0:64, :]))

            free_xb = (M, M.n)
            for s2 in range(2):
                prev_store = emit_ln_store(P, ctx, sout, g, s2, sems, prev_store, gxf)

    ctx["g_out_final"] = prev_store




def emit_l1_ffn(P, ctx):
    sb, ps = ctx["sb"], ctx["ps"]
    sin, sout = ctx["in3"], ctx["out3"]
    Lg, Ls = P.sem("gLg"), P.sem("gLs")
    M, V, C, SO = P.sem("gM"), P.sem("gV"), P.sem("gC"), P.sem("gSO")
    P.chain["vector"] = V
    sems = dict(M=M, V=V, C=C, SO=SO)

    free_xb = None
    prev_store = None
    g_p2_done = None
    g_p2_vec_done = None
    xb = lambda kc: sb["x_bf"][:, kc * G:(kc + 1) * G]
    yv = lambda oc: sb["y_sb"][:, oc * G:(oc + 1) * G]

    for g in range(NG):
        t0 = g * G
        g_prev_M = (M, M.n)
        g_prev_V = (V, V.n)
        P.w("gpsimd", free_xb)
        gxb = P.dma("gpsimd", lambda e, t0=t0: e.dma_start(
            out=sb3(sb["x_bf"], G), in_=rearr(sin[:, t0:t0 + G])))
        P.w("sync", (V, V.n))
        gxf = P.dma("sync", lambda e, t0=t0: e.dma_start(
            out=sb3(sb["y_sb"], G), in_=rearr(sin[:, t0:t0 + G])))
        P.w("scalar", g_prev_M)
        P.w("vector", g_prev_M)

        P.w("tensor", ctx["g_const"])
        P.w("tensor", ctx["g_vconst"])
        P.w("tensor", ctx["g_wcomb"])
        P.w("tensor", ctx["g_crow"])
        P.w("tensor", gxf)
        P.w("tensor", gxb)
        P.w("tensor", g_prev_V)
        last_gexp = None
        last_gtr = None
        for tt in range(4):
            if tt > 0:
                P.w("tensor", (V, V.n))
            for kc in range(8):
                P.i("tensor", lambda e, kc=kc, tt=tt: e.matmul(
                    ps["psU"][:, tt * 16:(tt + 1) * 16], yv(kc)[:, tt * 128:(tt + 1) * 128],
                    sb["wcomb"][:, kc * 16:(kc + 1) * 16],
                    start=(kc == 0), stop=False))
            gl = P.i("tensor", lambda e, tt=tt: e.matmul(
                ps["psU"][:, tt * 16:(tt + 1) * 16], sb["ones1"][:], sb["crow"][:],
                start=False, stop=True), inc=(M, 1))
            P.w("vector", gl)
            P.w("vector", last_gexp)
            P.i("vector", lambda e, tt=tt: e.tensor_copy(sb["l_a"][:], ps["psU"][:, tt * 16:(tt + 1) * 16]), inc=(V, 1))
            P.i("vector", lambda e: e.reduce_max(sb["col_a"][:], sb["l_a"][:], axis=AX.X), inc=(V, 1))
            cols = [sb["col_a"], sb["col_b"], sb["col_c"], sb["col_d"]]
            seq = [(sb["l_a"], sb["l_b"]), (sb["l_b"], sb["p8_a"]), (sb["p8_a"], sb["l_b"])]
            for r in range(1, 4):
                cur, nxt = seq[r - 1]
                P.i("vector", lambda e, cur=cur, r=r: e.tensor_scalar(
                    sb["p8_c"][:], cur[:], cols[r - 1][:, 0:1], None, ALU.is_ge), inc=(V, 1))
                P.i("vector", lambda e, cur=cur, nxt=nxt: e.scalar_tensor_tensor(
                    nxt[:], sb["p8_c"][:], -1e30, cur[:], ALU.mult, ALU.add), inc=(V, 1))
                P.i("vector", lambda e, nxt=nxt, r=r: e.reduce_max(cols[r][:], nxt[:], axis=AX.X), inc=(V, 1))
            gng = P.i("vector", lambda e: e.tensor_scalar_mul(sb["col_e"][:], sb["col_a"][:], -1.0), inc=(V, 1))
            P.w("scalar", gng)
            if DBG and g == 0:
                gdl = P.i("vector", lambda e, tt=tt: e.tensor_copy(sb["p8_a"][:], sb["l_a"][:]), inc=(V, 1))
                P.w("gpsimd", gdl)
                P.dma("gpsimd", lambda e, tt=tt: e.dma_start(out=ctx["dbg3_l"][:, tt * 16:(tt + 1) * 16], in_=sb["p8_a"][:], allow_slow_non_contiguous=True))
            gexp = P.i("scalar", lambda e: e.activation(
                sb["p8_b"][:], sb["l_a"][:], AF.Exp, bias=sb["col_e"][:, 0:1]), inc=(C, 1))
            last_gexp = gexp
            P.w("vector", gexp)
            P.i("vector", lambda e: e.tensor_scalar(
                sb["p8_c"][:], sb["l_a"][:], sb["col_d"][:, 0:1], None, ALU.is_ge), inc=(V, 1))
            P.i("vector", lambda e: e.tensor_mul(sb["p8_b"][:], sb["p8_b"][:], sb["p8_c"][:]), inc=(V, 1))
            P.i("vector", lambda e: e.reduce_sum(sb["col_e"][:], sb["p8_b"][:], axis=AX.X), inc=(V, 1))
            P.i("vector", lambda e: e.reciprocal(sb["col_f"][:], sb["col_e"][:]), inc=(V, 1))
            P.w("vector", last_gtr)
            gwt = P.i("vector", lambda e: e.tensor_scalar(
                sb["wte_bf"][:], sb["p8_b"][:], sb["col_f"][:, 0:1], None, ALU.mult), inc=(V, 1))
            P.w("tensor", gwt)
            gtr = P.i("tensor", lambda e: e.transpose(ps["psT"][0:16, 0:128], sb["wte_bf"][:], sb["ident_bf"][:]), inc=(M, 1))
            last_gtr = gtr
            P.w("vector", gtr)
            gwf = P.i("vector", lambda e, tt=tt: e.tensor_copy(
                sb["w_fm"][:, tt * 128:(tt + 1) * 128], ps["psT"][0:16, 0:128]), inc=(V, 1))
            P.w("tensor", gwf)
        g_wfm = (V, V.n)
        P.w("tensor", g_wfm)
        if DBG and g == 0:
            P.w("gpsimd", g_wfm)
            P.dma("gpsimd", lambda e: e.dma_start(out=ctx["dbg3_wfm"], in_=sb["w_fm"][:]))

        for e16 in range(16):
            P.w("scalar", g_p2_vec_done)
            gb = P.i("tensor", lambda e, e16=e16: e.matmul(
                ps["psU"][:], sb["sel16"][:, e16 * 128:(e16 + 1) * 128], sb["w_fm"][:],
                start=True, stop=True), inc=(M, 1))
            P.w("scalar", gb)
            gwbc = P.i("scalar", lambda e: e.copy(sb["wbc"][:], ps["psU"][:]), inc=(C, 1))
            P.w("tensor", gwbc)

            P.w("gpsimd", g_p2_done)
            gld_w1 = []
            for kc in range(8):
                gw1 = P.dma("gpsimd", lambda e, kc=kc, e16=e16: e.dma_start(
                    out=sb["g_w1"][:, kc * 4096:(kc + 1) * 4096],
                    in_=ctx["w1T"][e16, kc * 128:(kc + 1) * 128, :]))
                gld_w1.append(gw1)

            P.w("vector", g_p2_done)
            last_gge = {0: None, 1: None}
            last_gh = {0: None, 1: None}
            for fc in range(32):
                par = fc % 2
                bank = ps["psE"] if par == 0 else ps["psF"]
                P.w("tensor", last_gge[par])
                gm = None
                for kc in range(8):
                    if fc == 0:
                        P.w("tensor", gld_w1[kc])
                    gm = P.i("tensor", lambda e, kc=kc, fc=fc, bank=bank: e.matmul(
                        bank[:], sb["g_w1"][:, kc * 4096 + fc * 128: kc * 4096 + (fc + 1) * 128], xb(kc),
                        start=(kc == 0), stop=(kc == 7)), inc=(M, 1) if kc == 7 else None)
                P.w("scalar", gm)
                P.w("scalar", last_gh[par])
                gge = P.i("scalar", lambda e, fc=fc, e16=e16, bank=bank, par=par: e.activation(
                    sb["g_hg"][:, par * 512:(par + 1) * 512], bank[:], AF.Gelu,
                    bias=sb["b1_all"][:, e16 * 32 + fc: e16 * 32 + fc + 1]), inc=(C, 1))
                last_gge[par] = gge
                P.w("vector", gge)
                gh = P.i("vector", lambda e, fc=fc, par=par: e.tensor_mul(
                    sb["g_h"][:, fc * 512:(fc + 1) * 512],
                    sb["g_hg"][:, par * 512:(par + 1) * 512], sb["wbc"][:]), inc=(V, 1))
                last_gh[par] = gh
            g_p1_done = (M, M.n)
            g_p1_vec_done = (V, V.n)
            if DBG and g == 0 and e16 == 0:
                P.w("gpsimd", (C, C.n))
                P.dma("gpsimd", lambda e: e.dma_start(out=ctx["dbg3_wbc"], in_=sb["wbc"][:]))
                P.w("gpsimd", g_p1_vec_done)
                P.dma("gpsimd", lambda e: e.dma_start(out=ctx["dbg3_h"], in_=sb["g_h"][:]))

            P.w("gpsimd", g_p1_done)
            gld_w2 = []
            for fc in range(32):
                gw2 = P.dma("gpsimd", lambda e, fc=fc, e16=e16: e.dma_start(
                    out=sb["g_w1"][:, fc * 1024:(fc + 1) * 1024],
                    in_=ctx["w2T"][e16, fc * 128:(fc + 1) * 128, :]))
                gld_w2.append(gw2)

            P.w("tensor", g_p1_vec_done)
            banks = [ps["psA"], ps["psB"], ps["psC"], ps["psD"]]
            for r in range(2):
                gstop = [None] * 4
                if e16 == 0:
                    for j in range(4):
                        dc = r * 4 + j
                        P.i("tensor", lambda e, dc=dc, bank=banks[j]: e.matmul(
                            bank[:], sb["b2_bf"][:, dc * 128:(dc + 1) * 128], sb["w_fm"][:],
                            start=True, stop=False))
                for fc in range(32):
                    if r == 0:
                        P.w("tensor", gld_w2[fc])
                    for j in range(4):
                        dc = r * 4 + j
                        st_flag = (fc == 0) and (e16 != 0)
                        gstop[j] = P.i("tensor", lambda e, fc=fc, dc=dc, st=st_flag, bank=banks[j]: e.matmul(
                            bank[:], sb["g_w1"][:, fc * 1024 + dc * 128: fc * 1024 + (dc + 1) * 128],
                            sb["g_h"][:, fc * 512:(fc + 1) * 512],
                            start=st, stop=(fc == 31)), inc=(M, 1) if fc == 31 else None)
                gev = None
                for j in range(4):
                    dc = r * 4 + j
                    P.w("vector", gstop[j])
                    gev = P.i("vector", lambda e, dc=dc, bank=banks[j]: e.tensor_add(
                        yv(dc), yv(dc), bank[:]), inc=(V, 1))
                P.w("tensor", gev)
            g_p2_done = (M, M.n)
            g_p2_vec_done = (V, V.n)

        free_xb = (M, M.n)
        if DBG and g == 0:
            P.w("gpsimd", (V, V.n))
            P.dma("gpsimd", lambda e: e.dma_start(out=ctx["dbg3_z"], in_=sb["y_sb"][:]))
        for s2 in range(2):
            prev_store = emit_ln_store(P, ctx, sout, g, s2, sems, prev_store, gxf)

    ctx["g_out_final"] = prev_store


EMITTERS = {0: emit_l0_attn, 1: emit_l0_ffn, 2: emit_l1_attn, 3: emit_l1_ffn}




def _prep_consts():
    c = {}
    c["ident_d"] = np.eye(128, dtype=np.float32)
    c["mask_d"] = np.tril(np.ones((128, 128), dtype=np.float32))
    hsel = np.zeros((128, 128), dtype=np.float32)
    bsel = np.zeros((16, 1024), dtype=np.float32)
    for kc in range(8):
        for p in range(128):
            h = 2 * kc + (p // 64)
            hsel[p, kc * 16 + h] = 1.0
            bsel[h, kc * 128 + p] = 1.0
    c["hsel_d"] = hsel
    c["bsel_d"] = bsel
    sel8 = np.zeros((8, 1024), dtype=np.float32)
    for e in range(8):
        sel8[e, e * 128:(e + 1) * 128] = 1.0
    c["sel8_d"] = sel8
    sel16 = np.zeros((16, 2048), dtype=np.float32)
    for e in range(16):
        sel16[e, e * 128:(e + 1) * 128] = 1.0
    c["sel16_d"] = sel16
    return c


def _prep_weights(inp):
    T = lambda a: np.ascontiguousarray(np.asarray(a, dtype=np.float32).T)
    TE = lambda a: np.ascontiguousarray(np.asarray(a, dtype=np.float32).transpose(0, 2, 1))
    w = {}
    w["wqkvT"] = T(inp["L0_attn_w_attn"])
    w["wprojT"] = T(inp["L0_attn_w_proj"])
    w["routerT"] = T(inp["L0_ffn_router_w"])
    w["gateT"] = TE(inp["L0_ffn_gate_w"])
    w["upT"] = TE(inp["L0_ffn_up_w"])
    w["downT"] = TE(inp["L0_ffn_down_w"])
    w["wqT"] = T(inp["L1_attn_Wq"])
    w["wvT"] = T(inp["L1_attn_Wv"])
    qg = np.asarray(inp["L1_attn_q_norm_w"], dtype=np.float32).reshape(-1)
    w["qgam_d"] = np.ascontiguousarray(qg.reshape(8, 128).T)
    vg = np.asarray(inp["L1_attn_v_norm_w"], dtype=np.float32).reshape(1, 1024)
    w["vgam_d"] = np.ascontiguousarray(np.broadcast_to(vg, (128, 1024)))
    w["keysT"] = T(inp["L1_ffn_keys"])
    w["rwT"] = T(inp["L1_ffn_rw"])
    w["rb_d"] = np.ascontiguousarray(np.asarray(inp["L1_ffn_rb"], dtype=np.float32).reshape(1, 16))
    w["w1T"] = TE(inp["L1_ffn_w1"])
    w["w2T"] = TE(inp["L1_ffn_w2"])
    b1 = np.asarray(inp["L1_ffn_b1"], dtype=np.float32).reshape(16, 32, 128)
    w["b1_d"] = np.ascontiguousarray(b1.transpose(2, 0, 1).reshape(128, 512))
    w["b2_d"] = np.ascontiguousarray(np.asarray(inp["L1_ffn_b2"], dtype=np.float32))
    return w


_INPUTS_BY_BLOCK = {
    0: ["wqkvT", "wprojT"],
    1: ["routerT", "gateT", "upT", "downT", "sel8_d"],
    2: ["wqT", "wvT", "qgam_d", "vgam_d", "hsel_d", "bsel_d", "mask_d"],
    3: ["keysT", "rwT", "rb_d", "w1T", "w2T", "b1_d", "b2_d", "sel16_d"],
}

LAUNCH_PLAN = [[0], [1], [2], [3]]


def kernel(**inputs):
    consts = _prep_consts()
    weights = _prep_weights(inputs)
    x = np.asarray(inputs["L0_attn_x"], dtype=np.float32)
    shards = [np.ascontiguousarray(x[2 * c:2 * c + 2].reshape(TOK, 1024).T) for c in range(8)]

    cur = shards
    for blocks in LAUNCH_PLAN:
        nc = build_program(blocks)
        names = set(["ident_d"])
        for b in blocks:
            names.update(_INPUTS_BY_BLOCK[b])
        base = {n: (weights[n] if n in weights else consts[n]) for n in names}
        in_maps = []
        for c in range(8):
            m = dict(base)
            m["sin"] = cur[c]
            in_maps.append(m)
        res = run_bass_kernel_spmd(nc, in_maps, core_ids=list(range(8)))
        cur = [res.results[c]["sout"] for c in range(8)]

    out = np.empty((16, 2048, 1024), dtype=np.float32)
    for c in range(8):
        out[2 * c:2 * c + 2] = cur[c].T.reshape(2, 2048, 1024)
    return out
